# version 02
# baseline (speedup 1.0000x reference)
import numpy as np
import ml_dtypes
from contextlib import ExitStack

import concourse.bass as bass
import concourse.mybir as mybir
from concourse.bass_utils import run_bass_kernel_spmd

F32 = mybir.dt.float32
BF16 = mybir.dt.bfloat16
FP16 = mybir.dt.float16
AF = mybir.ActivationFunctionType
ALU = mybir.AluOpType

NCORES = 8
LN_EPS = 1e-5


class Cfg:
    def __init__(self, rows=2048, d=4096, i_dim=11008, st_rows=512, n_bis=35):
        self.ROWS = rows
        self.D = d
        self.I = i_dim
        self.ST = st_rows
        self.NST = rows // st_rows
        self.TPS = st_rows // 128
        self.NT = rows // 128
        self.HC = d // 128
        self.NI = i_dim // 128
        self.KEEP = i_dim // 2
        self.N_BIS = n_bis
        self.BIS_MID0 = 16.0
        jb = []
        off = 0
        while off < i_dim:
            w = min(512, i_dim - off)
            jb.append((off, w))
            off += w
        self.JB = jb
        gr = []
        off = 0
        while off < i_dim:
            w = min(1024, i_dim - off)
            gr.append((off, w))
            off += w
        self.GR = gr
        self.HBW = min(2048, d)
        self.NHH = d // self.HBW
        self.NHB = self.HBW // 512
        self.WJB = [(k * 512, 512) for k in range(d // 512)]
        self.QR = min(1024, d)
        self.ER = 256
        self.NE = self.ER // 128
        self.QTR = i_dim // 4


class Prog:
    ENGINES = ("sync", "vector", "scalar", "tensor", "gpsimd")
    QSEMS = {"sync": ("sWa", "sWb"), "gpsimd": ("sGa", "sGb")}

    def __init__(self, nc, ctx, semnames):
        self.nc = nc
        self.ops = {e: [] for e in self.ENGINES}
        self.sem = {}
        self.cnt = {}
        for n in semnames:
            self.sem[n] = ctx.enter_context(nc.semaphore(n))
            self.cnt[n] = 0
        self.qn = {"sync": 0, "gpsimd": 0}
        self.confirmed = {}

    def wait(self, eng, name, val=None):
        v = self.cnt[name] if val is None else val
        if v <= 0:
            return
        sem = self.sem[name]
        self.ops[eng].append(lambda e, _s=sem, _v=v: e.wait_ge(_s, _v))
        if eng in self.QSEMS and name in self.QSEMS[eng]:
            k = (eng, name)
            self.confirmed[k] = max(self.confirmed.get(k, 0), v)

    def waittok(self, eng, tok):
        if tok is not None:
            self.wait(eng, tok[0], tok[1])

    def qwait(self, eng, queue):
        for s in self.QSEMS[queue]:
            self.wait(eng, s)

    def qsnap(self, queue):
        return tuple((s, self.cnt[s]) for s in self.QSEMS[queue])

    def snapwait(self, eng, snap):
        if snap is not None:
            for s, v in snap:
                self.wait(eng, s, v)

    def op(self, eng, builder, inc=None, n=1):
        if inc is None:
            self.ops[eng].append(builder)
            return None
        sem = self.sem[inc]
        self.cnt[inc] += n
        self.ops[eng].append(lambda e, _b=builder, _s=sem, _n=n: _b(e).then_inc(_s, _n))
        return self.cnt[inc]

    def edge(self, eng, builder, sem):
        m = self.op(eng, builder, inc=sem)
        self.wait(eng, sem, m)
        return m

    def dma(self, eng, dst, src):
        sems = self.QSEMS[eng]
        s = sems[self.qn[eng] % 2]
        self.qn[eng] += 1
        k = (eng, s)
        pre = self.cnt[s]
        if pre > self.confirmed.get(k, 0):
            self.wait(eng, s, pre)
        val = self.op(eng, lambda e, _d=dst, _s=src: e.dma_start(_d, _s), inc=s, n=16)
        return (s, val)


def build_core_program(cfg, dbg=False):
    c = cfg
    nc = bass.Bass("TRN2", target_bir_lowering=False, detect_race_conditions=False)

    X = nc.dram_tensor("X", [c.ROWS, c.D], F32, kind="ExternalInput")
    WGT = nc.dram_tensor("WGT", [c.D, c.I], F32, kind="ExternalInput")
    WUT = nc.dram_tensor("WUT", [c.D, c.I], F32, kind="ExternalInput")
    WDT = nc.dram_tensor("WDT", [c.I, c.D], F32, kind="ExternalInput")
    WT = nc.dram_tensor("WT", [c.D, c.D], F32, kind="ExternalInput")
    BIAS = nc.dram_tensor("BIAS", [1, c.D], F32, kind="ExternalInput")
    ATT = nc.dram_tensor("ATT", [c.D, c.ER], F32, kind="ExternalInput")
    MTT = nc.dram_tensor("MTT", [c.ER, c.D], F32, kind="ExternalInput")
    PR = nc.dram_tensor("PR", [16], F32, kind="ExternalInput")
    IDT = nc.dram_tensor("IDT", [128, 128], BF16, kind="ExternalInput")
    OUT = nc.dram_tensor("OUT", [c.ROWS, c.D], F32, kind="ExternalOutput")

    ikind = "ExternalOutput" if dbg else "Internal"
    sD = nc.dram_tensor("sD", [c.ROWS, c.I], BF16, kind=ikind)
    qD = nc.dram_tensor("qD", [c.ROWS, c.I], FP16, kind=ikind)
    uD = nc.dram_tensor("uD", [c.ROWS, c.I], BF16, kind=ikind)
    x1D = nc.dram_tensor("x1D", [c.ROWS, c.D], F32, kind=ikind)

    with ExitStack() as ctx:
        e_ = ctx.enter_context
        U8 = mybir.dt.uint8
        A1 = e_(nc.sbuf_tensor("A1", [128, 69632], U8))
        A2 = e_(nc.sbuf_tensor("A2", [128, 40960], U8))
        A3 = e_(nc.sbuf_tensor("A3", [128, 56320], U8))

        def carve(arena, off, dt, shape):
            nb = {F32: 4, BF16: 2, FP16: 2}[dt] * int(np.prod(shape))
            ap = arena[:, off: off + nb].bitcast(dt)
            if len(shape) == 1:
                return ap
            letters = "abcd"[: len(shape)]
            spec = "p (" + " ".join(letters) + ") -> p " + " ".join(letters)
            return ap.rearrange(spec, **{letters[i]: shape[i] for i in range(len(shape) - 1)})

        pnl = carve(A1, 0, BF16, [2, c.HC, 512])
        mT = carve(A1, 0, BF16, [2, c.NI, 128])
        wds = carve(A1, 45056, F32, [2, c.HBW])
        wdb = carve(A1, 61440, BF16, [2, c.HBW])
        pstg = carve(A2, 0, F32, [2, 4, 512])
        sf32 = carve(A2, 16384, F32, [2, 512])
        s_stg = carve(A2, 20480, BF16, [c.TPS, 1024])
        q_stg = carve(A2, 28672, FP16, [c.TPS, 1024])
        xst = carve(A2, 0, F32, [2, c.D])
        ystg = carve(A2, 32768, BF16, [c.D])
        pr_s = carve(A2, 0, BF16, [2, 1024])
        pr_q = carve(A2, 4096, FP16, [2, 1024])
        pr_u = carve(A2, 8192, BF16, [2, 1024])
        pr_b = carve(A2, 12288, BF16, [1024])
        pr_m = carve(A2, 14336, BF16, [2, 1024])
        xres = carve(A2, 18432, F32, [2, 1024])
        x1stg = carve(A2, 26624, F32, [2, 1024])
        ostg = carve(A2, 16384, F32, [2, 512])
        hsb = carve(A2, 20480, BF16, [c.TPS, c.ER])
        hTb = carve(A2, 22528, BF16, [c.TPS, c.NE, 128])
        qt = carve(A3, 0, FP16, [2, c.I])
        junkV = carve(A3, 45056, BF16, [c.QTR])
        junkA = carve(A3, 50688, BF16, [c.QTR])
        atb = carve(A3, 0, BF16, [c.HC, c.ER])
        mtb = carve(A3, 16384, BF16, [c.NE, c.D])
        mstg = carve(A3, 32768, F32, [c.D])
        bstg = carve(A3, 0, F32, [c.D])
        yT = e_(nc.sbuf_tensor("yT", [128, c.HC, c.ST], BF16))
        stats = e_(nc.sbuf_tensor("stats", [128, c.D // 512, 6], F32))
        lnMV = e_(nc.sbuf_tensor("lnMV", [128, c.TPS, 2], F32))
        lnW = e_(nc.sbuf_tensor("lnW", [128, 5, c.TPS], F32))
        rstd = e_(nc.sbuf_tensor("rstd", [128, c.TPS], F32))
        thr = e_(nc.sbuf_tensor("thr", [128, c.NT], F32))
        bst = e_(nc.sbuf_tensor("bst", [128, 2, 12], F32))
        bbf = e_(nc.sbuf_tensor("bbf", [1, c.D], BF16))
        ones = e_(nc.sbuf_tensor("ones", [1, 128], BF16))
        pv = e_(nc.sbuf_tensor("pv", [128, 1], F32))
        idt = e_(nc.sbuf_tensor("idt", [128, 128], BF16))
        cst = e_(nc.sbuf_tensor("cst", [128, 2], F32))
        ps = e_(nc.psum_tensor("ps", [128, 8, 512], F32))

        psTR = [ps[:, 6, :].bitcast(BF16).rearrange("p (s f) -> p s f", s=8),
                ps[:, 7, :].bitcast(BF16).rearrange("p (s f) -> p s f", s=8)]
        psA = ps[:, 4, 0:c.ER]

        yTdbg = nc.dram_tensor("yTdbg", [128, c.HC * c.ST], BF16,
                               kind="ExternalOutput") if dbg else None
        upDbg = nc.dram_tensor("upDbg", [128, 2048], F32,
                               kind="ExternalOutput") if dbg else None

        prog = Prog(nc, ctx, ["sWa", "sWb", "sGa", "sGb", "cV", "cA", "cT", "cP", "cB"])
        P = prog
        DLT = 2.0 * c.BIS_MID0 / float(2 ** c.N_BIS)

        tok_b = P.dma("sync", bstg[0:1, :], BIAS.ap())
        P.dma("gpsimd", idt[:], IDT.ap())
        for k in range(8):
            P.dma("gpsimd", pv[16 * k:16 * (k + 1), 0:1], PR.ap()[0:16])
        P.waittok("vector", tok_b)
        P.qwait("vector", "gpsimd")
        P.op("vector", lambda e: e.memset(ones[:], 1.0))
        P.op("vector", lambda e: e.memset(cst[:, 0:1], c.BIS_MID0))
        P.op("vector", lambda e: e.memset(cst[:, 1:2], -DLT))
        P.op("vector", lambda e: e.tensor_copy(bbf[:], bstg[0:1, :]), inc="cV")
        prologue_cV = P.cnt["cV"]
        P.wait("scalar", "cV", prologue_cV)
        P.qwait("tensor", "gpsimd")
        P.wait("tensor", "cV", prologue_cV)

        trst = {"grp": 0, "marks": []}

        def transpose_blocks(srcs, dst_ap):
            g = trst["grp"]
            bank = g % 2
            base = 4 * ((g // 2) % 2)
            if len(trst["marks"]) >= 2:
                P.wait("tensor", "cA", trst["marks"][-2])
            nb = len(srcs)
            for bi, src in enumerate(srcs):
                P.op("tensor",
                     lambda e, _s=src, _d=psTR[bank][:, base + bi, :]: e.transpose(_d, _s, idt[:]),
                     inc=("cT" if bi == nb - 1 else None))
            P.wait("scalar", "cT")
            mark = P.op("scalar",
                        lambda e, _d=dst_ap, _sl=psTR[bank][:, base:base + nb, :]: e.copy(_d, _sl),
                        inc="cA")
            trst["grp"] += 1
            trst["marks"].append(mark)

        xst_guard = [None, None]
        ystg_guard = [None]
        x1_done = {"snap": None}

        def emit_ln(st, src_dram):
            r0 = st * c.ST
            P.wait("gpsimd", "cT")
            P.wait("gpsimd", "cV")
            P.wait("gpsimd", "cA")
            P.snapwait("gpsimd", x1_done["snap"])
            for ch in range(c.TPS):
                buf = ch % 2
                if xst_guard[buf] is not None:
                    P.wait("gpsimd", "cV", xst_guard[buf])
                tok = P.dma("gpsimd", xst[:, buf, :],
                            src_dram.ap()[r0 + 128 * ch: r0 + 128 * (ch + 1), :])
                P.waittok("vector", tok)
                P.wait("vector", "cV")
                for u in range(c.D // 512):
                    P.op("vector", lambda e, _u=u, _b=buf: e.bn_stats(
                        stats[:, _u, :], xst[:, _b, 512 * _u:512 * (_u + 1)]),
                        inc=("cV" if u == c.D // 512 - 1 else None))
                P.wait("vector", "cV")
                xst_guard[buf] = P.op("vector", lambda e, _ch=ch: e.bn_aggr(
                    lnMV[:, _ch, :], stats[:].rearrange("p a b -> p (a b)")), inc="cV")
            w_, r_, t1, t2, t3 = (lnW[:, k, :] for k in range(5))
            P.wait("vector", "cV")
            P.edge("vector", lambda e: e.tensor_scalar(w_, lnMV[:, :, 1], LN_EPS, None, ALU.add), "cV")
            P.edge("vector", lambda e: e.memset(r_, 0.05), "cV")
            for _ in range(18):
                P.edge("vector", lambda e: e.tensor_mul(t1, r_, r_), "cV")
                P.edge("vector", lambda e: e.tensor_mul(t2, t1, w_), "cV")
                P.edge("vector", lambda e: e.tensor_scalar(t3, t2, -0.5, 1.5, ALU.mult, ALU.add), "cV")
                P.edge("vector", lambda e: e.tensor_mul(r_, r_, t3), "cV")
            rstd_mark = P.edge("vector", lambda e: e.tensor_copy(rstd[:], r_), "cV")
            for ch in range(c.TPS):
                buf = ch % 2
                if xst_guard[buf] is not None:
                    P.wait("gpsimd", "cV", xst_guard[buf])
                tok = P.dma("gpsimd", xst[:, buf, :],
                            src_dram.ap()[r0 + 128 * ch: r0 + 128 * (ch + 1), :])
                P.waittok("vector", tok)
                P.wait("vector", "cV", rstd_mark)
                if ystg_guard[0] is not None:
                    P.wait("vector", "cT", ystg_guard[0])
                xst_guard[buf] = P.op("vector", lambda e, _ch=ch, _b=buf: e.tensor_scalar(
                    ystg[:], xst[:, _b, :], lnMV[:, _ch, 0:1], rstd[:, _ch:_ch + 1],
                    ALU.subtract, ALU.mult), inc="cV")
                P.wait("tensor", "cV", xst_guard[buf])
                for h0 in range(0, c.HC, 4):
                    nb = min(4, c.HC - h0)
                    srcs = [ystg[:, 128 * (h0 + bi):128 * (h0 + bi + 1)] for bi in range(nb)]
                    transpose_blocks(srcs, yT[:, h0:h0 + nb, 128 * ch:128 * (ch + 1)])
                ystg_guard[0] = P.cnt["cT"]

        pstg_guard = {}
        panel_done = {}
        sf32_guard = [None, None]
        stg_guard = {}

        def emit_proj(st, w_dram, kind):
            r0 = st * c.ST
            it = [0]
            slot_guard = {}
            P.wait("tensor", "cA")
            P.wait("tensor", "cV")
            P.wait("sync", "cA")
            P.wait("sync", "cV")
            P.wait("sync", "cT")
            P.wait("gpsimd", "cT")
            for jidx, (j0, jw) in enumerate(c.JB):
                pbuf = jidx % 2
                if panel_done.get(pbuf) is not None:
                    P.wait("gpsimd", "cT", panel_done[pbuf])
                nqd = c.D // 512
                for qd in range(nqd):
                    sl = qd % 2
                    if pstg_guard.get(sl) is not None:
                        P.wait("sync", "cP", pstg_guard[sl])
                    rr0 = qd * 512
                    tok = P.dma("sync", pstg[:, sl, :, 0:jw],
                                w_dram.ap()[rr0:rr0 + 512, j0:j0 + jw]
                                .rearrange("(a p) n -> p a n", p=128))
                    P.waittok("gpsimd", tok)
                    pstg_guard[sl] = P.op(
                        "gpsimd", lambda e, _s=sl, _pb=pbuf, _r=rr0 // 128:
                        e.tensor_copy(pnl[:, _pb, _r:_r + 4, :],
                                      pstg[:, _s, :, :]), inc="cP")
                P.wait("tensor", "cP", pstg_guard[(nqd - 1) % 2])
                for t in range(c.TPS):
                    sl4 = it[0] % 4
                    fb = it[0] % 2
                    it[0] += 1
                    if sf32_guard[fb] is not None:
                        P.wait("scalar", "cV", sf32_guard[fb])
                    if slot_guard.get(sl4) is not None:
                        P.wait("tensor", "cA", slot_guard[sl4])
                    mm_mark = None
                    for h in range(c.HC):
                        mm_mark = P.op("tensor", lambda e, _h=h, _t=t, _pb=pbuf, _jw=jw, _sl=sl4:
                                       e.matmul(ps[:, _sl, 0:_jw],
                                                yT[:, _h, 128 * _t:128 * (_t + 1)],
                                                pnl[:, _pb, _h, 0:_jw],
                                                start=(_h == 0), stop=(_h == c.HC - 1)),
                                       inc=("cT" if h == c.HC - 1 else None))
                    panel_done[pbuf] = mm_mark
                    P.wait("scalar", "cT", mm_mark)
                    gidx = next(k for k, (g0, gw) in enumerate(c.GR) if g0 <= j0 < g0 + gw)
                    g0, gw = c.GR[gidx]
                    soff = j0 - g0
                    if stg_guard.get(t) is not None and soff == 0:
                        P.waittok("scalar", stg_guard[t])
                        if kind == "gate":
                            P.waittok("vector", stg_guard[t])
                    if kind == "gate":
                        P.op("scalar", lambda e, _sl=sl4, _jw=jw, _fb=fb: e.activation(
                            sf32[:, _fb, 0:_jw], ps[:, _sl, 0:_jw], AF.Silu))
                        q_mark = P.op("scalar", lambda e, _t=t, _so=soff, _jw=jw, _fb=fb:
                                      e.activation(q_stg[:, _t, _so:_so + _jw],
                                                   sf32[:, _fb, 0:_jw], AF.Abs), inc="cA")
                        P.wait("vector", "cA", q_mark)
                        sf32_guard[fb] = P.op(
                            "vector", lambda e, _t=t, _so=soff, _jw=jw, _fb=fb:
                            e.tensor_copy(s_stg[:, _t, _so:_so + _jw], sf32[:, _fb, 0:_jw]),
                            inc="cV")
                        evac_done = ("cV", sf32_guard[fb])
                        slot_guard[sl4] = q_mark
                    else:
                        m_ = P.op("scalar", lambda e, _sl=sl4, _t=t, _so=soff: e.copy(
                            s_stg[:, _t, _so:_so + 512], ps[:, _sl, :]), inc="cA")
                        evac_done = ("cA", m_)
                        slot_guard[sl4] = m_
                        if dbg and st == 0 and t == 0 and j0 == 10752:
                            P.wait("gpsimd", "cV", sf32_guard[fb])
                            P.dma("gpsimd", upDbg.ap()[:, 0:256], sf32[:, fb, 0:256])
                            P.dma("gpsimd", upDbg.ap()[:, 256:1024],
                                  s_stg[:, 0, 0:768].bitcast(F32).bitcast(BF16))
                            P.dma("gpsimd", upDbg.ap()[:, 1024:1536],
                                  pnl[:, pbuf, 0:2, :].rearrange("p a b -> p (a b)").bitcast(F32))
                            P.dma("gpsimd", upDbg.ap()[:, 1536:1792],
                                  pstg[:, (c.D // 512 - 1) % 2, 0, 0:256])
                    if j0 + jw == g0 + gw:
                        P.wait("gpsimd", evac_done[0], evac_done[1])
                        dst = sD if kind == "gate" else uD
                        tk1 = P.dma("gpsimd", dst.ap()[r0 + 128 * t:r0 + 128 * (t + 1), g0:g0 + gw],
                                    s_stg[:, t, 0:gw])
                        if kind == "gate":
                            stg_guard[t] = P.dma(
                                "gpsimd", qD.ap()[r0 + 128 * t:r0 + 128 * (t + 1), g0:g0 + gw],
                                q_stg[:, t, 0:gw])
                            stg_guard[(t, "s")] = tk1
                        else:
                            stg_guard[t] = tk1

        bis_done = {}
        qt_guard = [None, None]

        def emit_bisect(tile, eng):
            Q = c.QTR
            k_thr = float(c.KEEP) - 0.5
            if qt_guard[eng] is not None:
                P.wait("gpsimd", "cB", qt_guard[eng])
            P.wait("gpsimd", "cT")
            P.waittok("gpsimd", stg_guard.get(tile % c.TPS))
            P.waittok("gpsimd", stg_guard.get((tile % c.TPS, "s")))
            tok_q = P.dma("gpsimd", qt[:, eng, :], qD.ap()[128 * tile:128 * (tile + 1), :])
            if eng == 0:
                mid = bst[:, 0, 0:1]
                cn = [bst[:, 0, kk:kk + 1] for kk in (1, 2, 3, 4)]
                s1 = bst[:, 0, 5:6]
                s2 = bst[:, 0, 6:7]
                dd = bst[:, 0, 7:8]
                P.waittok("vector", tok_q)
                for k in range(c.N_BIS):
                    ck = c.BIS_MID0 / float(2 ** (k + 1))
                    sc = c.BIS_MID0 if k == 0 else mid
                    if k > 0:
                        P.wait("vector", "cV")
                    for qq in range(4):
                        P.op("vector", lambda e, _sc=sc, _q=qq: e.tensor_scalar(
                            junkV[:], qt[:, 0, Q * _q:Q * (_q + 1)], _sc, 0.0,
                            ALU.is_ge, ALU.add, accum_out=cn[_q]),
                            inc=("cV" if qq == 3 else None))
                    P.wait("vector", "cV")
                    P.edge("vector", lambda e: e.tensor_add(s1, cn[0], cn[1]), "cV")
                    P.edge("vector", lambda e: e.tensor_add(s2, cn[2], cn[3]), "cV")
                    P.edge("vector", lambda e: e.tensor_add(s1, s1, s2), "cV")
                    P.edge("vector", lambda e, _ck=ck: e.tensor_scalar(
                        dd, s1, k_thr, 2.0 * _ck, ALU.is_ge, ALU.mult), "cV")
                    if k == 0:
                        P.op("vector", lambda e, _ck=ck: e.tensor_scalar(
                            mid, dd, c.BIS_MID0 - _ck, None, ALU.add), inc="cV")
                    else:
                        P.op("vector", lambda e, _ck=ck: e.scalar_tensor_tensor(
                            mid, dd, -_ck, mid, ALU.add, ALU.add), inc="cV")
                P.wait("vector", "cV")
                mark = P.op("vector", lambda e, _t=tile: e.tensor_scalar(
                    thr[:, _t:_t + 1], mid, -DLT, None, ALU.add), inc="cB")
            else:
                amA = bst[:, 1, 0:1]
                amB = bst[:, 1, 1:2]
                aa = [bst[:, 1, kk:kk + 1] for kk in (2, 3, 4, 5)]
                tsum = bst[:, 1, 6:7]
                pp = bst[:, 1, 7:8]
                P.waittok("scalar", tok_q)
                cur, nxt = amA, amB
                for k in range(c.N_BIS):
                    ck = c.BIS_MID0 / float(2 ** (k + 1))
                    bias = cst[:, 0:1] if k == 0 else cur
                    if k > 0:
                        P.wait("scalar", "cA")
                    for qq in range(4):
                        P.op("scalar", lambda e, _b=bias, _q=qq: e.activation(
                            junkA[:], qt[:, 1, Q * _q:Q * (_q + 1)], AF.Sign, bias=_b,
                            scale=-1.0, accum_out=aa[_q]),
                            inc=("cA" if qq == 3 else None))
                    P.wait("scalar", "cA")
                    P.edge("scalar", lambda e: e.activation(
                        tsum, aa[1], AF.Identity, bias=aa[0], scale=1.0), "cA")
                    P.edge("scalar", lambda e: e.activation(
                        tsum, aa[2], AF.Identity, bias=tsum, scale=1.0), "cA")
                    P.edge("scalar", lambda e: e.activation(
                        tsum, aa[3], AF.Identity, bias=tsum, scale=1.0), "cA")
                    P.edge("scalar", lambda e: e.activation(
                        pp, tsum, AF.Sign, scale=-1.0), "cA")
                    bias2 = cst[:, 0:1] if k == 0 else cur
                    P.edge("scalar", lambda e, _b=bias2, _n=nxt, _ck=ck: e.activation(
                        _n, pp, AF.Identity, bias=_b, scale=_ck), "cA")
                    cur, nxt = nxt, cur
                mark = P.op("scalar", lambda e, _t=tile, _c=cur: e.activation(
                    thr[:, _t:_t + 1], _c, AF.Identity, bias=cst[:, 1:2], scale=1.0), inc="cB")
            bis_done[tile] = mark
            qt_guard[eng] = mark

        pr_guard = [None, None]
        pr_m_guard = {}

        def emit_product(tile):
            slot = tile % 2
            P.wait("gpsimd", "cP")
            P.waittok("gpsimd", stg_guard.get(tile % c.TPS))
            P.waittok("gpsimd", stg_guard.get((tile % c.TPS, "s")))
            for gidx, (g0, gw) in enumerate(c.GR):
                buf = gidx % 2
                if pr_guard[buf] is not None:
                    P.wait("gpsimd", "cV", pr_guard[buf])
                tk1 = P.dma("gpsimd", pr_s[:, buf, 0:gw],
                            sD.ap()[128 * tile:128 * (tile + 1), g0:g0 + gw])
                tk2 = P.dma("gpsimd", pr_q[:, buf, 0:gw],
                            qD.ap()[128 * tile:128 * (tile + 1), g0:g0 + gw])
                tk3 = P.dma("gpsimd", pr_u[:, buf, 0:gw],
                            uD.ap()[128 * tile:128 * (tile + 1), g0:g0 + gw])
                for tk in (tk1, tk2, tk3):
                    P.waittok("vector", tk)
                P.wait("vector", "cB", bis_done[tile])
                if pr_m_guard.get(buf) is not None:
                    P.wait("vector", "cT", pr_m_guard[buf])
                P.op("vector", lambda e, _b=buf, _gw=gw, _t=tile: e.tensor_scalar(
                    pr_b[:, 0:_gw], pr_q[:, _b, 0:_gw], thr[:, _t:_t + 1], None, ALU.is_ge))
                P.op("vector", lambda e, _b=buf, _gw=gw: e.tensor_mul(
                    pr_m[:, _b, 0:_gw], pr_b[:, 0:_gw], pr_s[:, _b, 0:_gw]))
                pr_guard[buf] = P.op("vector", lambda e, _b=buf, _gw=gw: e.tensor_mul(
                    pr_m[:, _b, 0:_gw], pr_m[:, _b, 0:_gw], pr_u[:, _b, 0:_gw]), inc="cV")
                P.wait("tensor", "cV", pr_guard[buf])
                nblk = gw // 128
                for b0 in range(0, nblk, 4):
                    nb = min(4, nblk - b0)
                    srcs = [pr_m[:, buf, 128 * (b0 + bi):128 * (b0 + bi + 1)]
                            for bi in range(nb)]
                    ic0 = (g0 // 128) + b0
                    transpose_blocks(srcs, mT[:, slot, ic0:ic0 + nb, :])
                pr_m_guard[buf] = P.cnt["cT"]

        wds_guard = {}
        wdb_guard = {}
        xres_guard = [None, None]
        x1_guard = [None, None]

        def emit_down_pair(st, pair):
            tiles = [t for t in (2 * pair, 2 * pair + 1) if t < c.TPS]
            ntl = len(tiles)
            P.wait("tensor", "cA")
            P.wait("tensor", "cV")
            P.wait("sync", "cT")
            for hh in range(c.NHH):
                hc0 = hh * c.HBW
                last_mark = None
                for i in range(c.NI):
                    sl = i % 2
                    if wds_guard.get(sl) is not None:
                        P.wait("sync", "cP", wds_guard[sl])
                    tok = P.dma("sync", wds[:, sl, :],
                                WDT.ap()[128 * i:128 * (i + 1), hc0:hc0 + c.HBW])
                    P.waittok("gpsimd", tok)
                    if wdb_guard.get(sl) is not None:
                        P.wait("gpsimd", "cT", wdb_guard[sl])
                    wds_guard[sl] = P.op("gpsimd", lambda e, _s=sl: e.tensor_copy(
                        wdb[:, _s, :], wds[:, _s, :]), inc="cP")
                    P.wait("tensor", "cP", wds_guard[sl])
                    for ti in range(ntl):
                        for hb in range(c.NHB):
                            bank = ti * c.NHB + hb
                            grp_last = (ti == ntl - 1 and hb == c.NHB - 1)
                            last_mark = P.op(
                                "tensor",
                                lambda e, _i=i, _sl=(tiles[ti] % 2), _b=bank, _s=sl, _hb=hb,
                                _f=(i == 0), _la=(i == c.NI - 1):
                                e.matmul(ps[:, _b, :], mT[:, _sl, _i, :],
                                         wdb[:, _s, 512 * _hb:512 * (_hb + 1)],
                                         start=_f, stop=_la),
                                inc=("cT" if grp_last else None))
                    wdb_guard[sl] = last_mark
                P.wait("vector", "cT", last_mark)
                W1 = min(1024, c.HBW)
                nhalf = c.HBW // W1
                nb2 = W1 // 512
                for ti in range(ntl):
                    tile = tiles[ti]
                    gtile = st * c.TPS + tile
                    for half in range(nhalf):
                        xb = half % 2
                        if xres_guard[xb] is not None:
                            P.wait("gpsimd", "cV", xres_guard[xb])
                        tok = P.dma("gpsimd", xres[:, xb, 0:W1],
                                    X.ap()[128 * gtile:128 * (gtile + 1),
                                           hc0 + W1 * half:hc0 + W1 * (half + 1)])
                        P.waittok("vector", tok)
                        P.waittok("vector", x1_guard[xb])
                        for hb2 in range(nb2):
                            bank = ti * c.NHB + nb2 * half + hb2
                            xres_guard[xb] = P.op(
                                "vector", lambda e, _b=bank, _xb=xb, _h=hb2: e.tensor_add(
                                    x1stg[:, _xb, 512 * _h:512 * (_h + 1)],
                                    ps[:, _b, :], xres[:, _xb, 512 * _h:512 * (_h + 1)]),
                                inc=("cV" if hb2 == nb2 - 1 else None))
                        P.wait("gpsimd", "cV", xres_guard[xb])
                        x1_guard[xb] = P.dma(
                            "gpsimd",
                            x1D.ap()[128 * gtile:128 * (gtile + 1),
                                     hc0 + W1 * half:hc0 + W1 * (half + 1)],
                            x1stg[:, xb, 0:W1])

        mstg_guard = [None]
        ostg_guard = [None, None]

        def emit_ffn(st):
            P.snapwait("gpsimd", x1_done["snap"])
            P.wait("tensor", "cA")
            P.wait("tensor", "cV")
            P.wait("vector", "cT")
            P.wait("sync", "cB")
            P.wait("sync", "cT")
            P.wait("sync", "cV")
            P.wait("sync", "cA")
            for hcb in range(c.HC):
                if mstg_guard[0] is not None:
                    P.wait("sync", "cV", mstg_guard[0])
                tok = P.dma("sync", mstg[:, 0:c.ER], ATT.ap()[128 * hcb:128 * (hcb + 1), :])
                P.waittok("vector", tok)
                mstg_guard[0] = P.op("vector", lambda e, _h=hcb: e.tensor_copy(
                    atb[:, _h, :], mstg[:, 0:c.ER]), inc="cV")
            for ech in range(c.NE):
                if mstg_guard[0] is not None:
                    P.wait("sync", "cV", mstg_guard[0])
                tok = P.dma("sync", mstg[:], MTT.ap()[128 * ech:128 * (ech + 1), :])
                P.waittok("vector", tok)
                mstg_guard[0] = P.op("vector", lambda e, _e=ech: e.tensor_scalar(
                    mtb[:, _e, :], mstg[:], pv[:, 0:1], None, ALU.mult), inc="cV")
            at_mark = P.cnt["cV"]
            P.wait("tensor", "cV", at_mark)
            mmk = None
            for t in range(c.TPS):
                if mmk is not None:
                    P.wait("tensor", "cA")
                for h in range(c.HC):
                    mmk = P.op("tensor", lambda e, _h=h, _t=t: e.matmul(
                        psA, yT[:, _h, 128 * _t:128 * (_t + 1)], atb[:, _h, :],
                        start=(_h == 0), stop=(_h == c.HC - 1)),
                        inc=("cT" if h == c.HC - 1 else None))
                P.wait("scalar", "cT", mmk)
                hm = P.op("scalar", lambda e, _t=t: e.copy(hsb[:, _t, :], psA), inc="cA")
                P.wait("tensor", "cA", hm)
                srcs = [hsb[:, t, 128 * eb:128 * (eb + 1)] for eb in range(c.NE)]
                transpose_blocks(srcs, hTb[:, t, :, :])
            hT_mark = trst["marks"][-1]
            slot_mark = [None] * 4
            it = [0]
            for jidx, (j0, jw) in enumerate(c.WJB):
                pbuf = jidx % 2
                if panel_done.get(pbuf) is not None:
                    P.wait("gpsimd", "cT", panel_done[pbuf])
                nqd = c.D // 512
                for qd in range(nqd):
                    sl = qd % 2
                    if pstg_guard.get(sl) is not None:
                        P.wait("sync", "cP", pstg_guard[sl])
                    rr0 = qd * 512
                    tok = P.dma("sync", pstg[:, sl, :, 0:jw],
                                WT.ap()[rr0:rr0 + 512, j0:j0 + jw]
                                .rearrange("(a p) n -> p a n", p=128))
                    P.waittok("gpsimd", tok)
                    pstg_guard[sl] = P.op(
                        "gpsimd", lambda e, _s=sl, _pb=pbuf, _r=rr0 // 128, _jw=jw:
                        e.tensor_copy(pnl[:, _pb, _r:_r + 4, 0:_jw],
                                      pstg[:, _s, :, 0:_jw]), inc="cP")
                P.wait("tensor", "cP", pstg_guard[(nqd - 1) % 2])
                P.wait("tensor", "cA", hT_mark)
                for t in range(c.TPS):
                    gtile = st * c.TPS + t
                    sl4 = it[0] % 4
                    ob = it[0] % 2
                    it[0] += 1
                    if slot_mark[sl4] is not None:
                        P.wait("tensor", "cV", slot_mark[sl4])
                    for h in range(c.HC):
                        P.op("tensor", lambda e, _h=h, _t=t, _pb=pbuf, _sl=sl4, _jw=jw:
                             e.matmul(ps[:, _sl, 0:_jw], yT[:, _h, 128 * _t:128 * (_t + 1)],
                                      pnl[:, _pb, _h, 0:_jw], start=(_h == 0), stop=False))
                    P.op("tensor", lambda e, _sl=sl4, _j0=j0, _jw=jw: e.matmul(
                        ps[:, _sl, 0:_jw], ones[:], bbf[:, _j0:_j0 + _jw],
                        start=False, stop=False))
                    mm_mark = None
                    for ech in range(c.NE):
                        lastc = ech == c.NE - 1
                        mm_mark = P.op(
                            "tensor",
                            lambda e, _e=ech, _t=t, _sl=sl4, _j0=j0, _jw=jw, _la=lastc:
                            e.matmul(ps[:, _sl, 0:_jw], hTb[:, _t, _e, :],
                                     mtb[:, _e, _j0:_j0 + _jw], start=False, stop=_la),
                            inc=("cT" if lastc else None))
                    panel_done[pbuf] = mm_mark
                    P.waittok("gpsimd", ostg_guard[ob])
                    tok = P.dma("gpsimd", ostg[:, ob, 0:jw],
                                x1D.ap()[128 * gtile:128 * (gtile + 1), j0:j0 + jw])
                    P.waittok("vector", tok)
                    P.wait("vector", "cT", mm_mark)
                    slot_mark[sl4] = P.op("vector", lambda e, _sl=sl4, _ob=ob, _jw=jw:
                                          e.tensor_add(ostg[:, _ob, 0:_jw], ps[:, _sl, 0:_jw],
                                                       ostg[:, _ob, 0:_jw]), inc="cV")
                    P.wait("gpsimd", "cV", slot_mark[sl4])
                    ostg_guard[ob] = P.dma(
                        "gpsimd", OUT.ap()[128 * gtile:128 * (gtile + 1), j0:j0 + jw],
                        ostg[:, ob, 0:jw])

        for st in range(c.NST):
            x1_done["snap"] = None
            emit_ln(st, X)
            if dbg and st == 0:
                P.wait("gpsimd", "cA")
                P.dma("gpsimd", yTdbg.ap(), yT[:].rearrange("p a b -> p (a b)"))
            emit_proj(st, WGT, "gate")
            emit_proj(st, WUT, "up")
            for tl in range(c.TPS):
                emit_bisect(st * c.TPS + tl, tl % 2)
            for pair in range((c.TPS + 1) // 2):
                for tl in (2 * pair, 2 * pair + 1):
                    if tl < c.TPS:
                        emit_product(st * c.TPS + tl)
                emit_down_pair(st, pair)
            x1_done["snap"] = P.qsnap("gpsimd")
            emit_ln(st, x1D)
            emit_ffn(st)

        P.qwait("gpsimd", "gpsimd")
        P.qwait("gpsimd", "sync")

        with nc.Block() as block:
            @block.sync
            def _(eng):
                for f in prog.ops["sync"]:
                    f(eng)

            @block.vector
            def _(eng):
                for f in prog.ops["vector"]:
                    f(eng)

            @block.scalar
            def _(eng):
                for f in prog.ops["scalar"]:
                    f(eng)

            @block.tensor
            def _(eng):
                for f in prog.ops["tensor"]:
                    f(eng)

            @block.gpsimd
            def _(eng):
                for f in prog.ops["gpsimd"]:
                    f(eng)

    return nc


def _prep_inputs(L0_attn_x, L0_attn_w_gate, L0_attn_w_up, L0_attn_w_down,
                 L0_ffn_W, L0_ffn_b, L0_ffn_A, L0_ffn_B, L0_ffn_probs, cfg):
    c = cfg
    x = np.ascontiguousarray(np.asarray(L0_attn_x, dtype=np.float32).reshape(-1, c.D))
    shared = {
        "WGT": np.ascontiguousarray(np.asarray(L0_attn_w_gate, np.float32).T),
        "WUT": np.ascontiguousarray(np.asarray(L0_attn_w_up, np.float32).T),
        "WDT": np.ascontiguousarray(np.asarray(L0_attn_w_down, np.float32).T),
        "WT": np.ascontiguousarray(np.asarray(L0_ffn_W, np.float32).T),
        "BIAS": np.asarray(L0_ffn_b, np.float32).reshape(1, -1),
        "ATT": np.ascontiguousarray(
            np.asarray(L0_ffn_A, np.float32).transpose(1, 0, 2).reshape(c.ER, c.D).T),
        "MTT": np.ascontiguousarray(
            np.asarray(L0_ffn_B, np.float32).transpose(2, 0, 1).reshape(c.ER, c.D)),
        "PR": np.asarray(L0_ffn_probs, np.float32),
        "IDT": np.eye(128, dtype=ml_dtypes.bfloat16),
    }
    return x, shared


def kernel(L0_attn_x, L0_attn_w_gate, L0_attn_w_up, L0_attn_w_down,
           L0_ffn_W, L0_ffn_b, L0_ffn_A, L0_ffn_B, L0_ffn_probs):
    cfg = Cfg()
    x, shared = _prep_inputs(L0_attn_x, L0_attn_w_gate, L0_attn_w_up, L0_attn_w_down,
                             L0_ffn_W, L0_ffn_b, L0_ffn_A, L0_ffn_B, L0_ffn_probs, cfg)
    nc = build_core_program(cfg)
    in_maps = []
    for core in range(NCORES):
        m = dict(shared)
        m["X"] = x[core * cfg.ROWS:(core + 1) * cfg.ROWS]
        in_maps.append(m)
    res = run_bass_kernel_spmd(nc, in_maps, core_ids=list(range(NCORES)))
    out = np.concatenate([res.results[cc]["OUT"] for cc in range(NCORES)], axis=0)
    return out.reshape(np.asarray(L0_attn_x).shape).astype(np.float32)


# version 03
# speedup vs baseline: 1.8457x; 1.8457x over previous
import numpy as np
import ml_dtypes
from contextlib import ExitStack

import concourse.bass as bass
import concourse.mybir as mybir
from concourse.bass_utils import run_bass_kernel_spmd

F32 = mybir.dt.float32
BF16 = mybir.dt.bfloat16
I16 = mybir.dt.int16
FP16 = mybir.dt.float16
U8 = mybir.dt.uint8
AF = mybir.ActivationFunctionType
ALU = mybir.AluOpType

NCORES = 8
LN_EPS = 1e-5


class Cfg:
    def __init__(self, rows=2048, d=4096, i_dim=11008):
        self.ROWS = rows
        self.D = d
        self.I = i_dim
        self.NT = rows // 128
        self.HC = d // 128
        self.NI = i_dim // 128
        self.KEEP = i_dim // 2
        self.N_BIS = 15
        jb = []
        off = 0
        while off < i_dim:
            w = min(512, i_dim - off)
            jb.append((off, w))
            off += w
        self.JB = jb
        self.WJB = [(k * 512, 512) for k in range(d // 512)]
        base = self.NI // 4
        rem = self.NI - 4 * base
        qs = []
        c0 = 0
        for k in range(4):
            n = base + (1 if k < rem else 0)
            qs.append((c0, n))
            c0 += n
        self.QS = qs
        self.QNIMX = max(n for _, n in qs)
        self.ER = 256
        self.NE = self.ER // 128
        self.NHALF = i_dim // 2
        self.QW = self.NHALF // 4


class Prog:
    ENGINES = ("sync", "vector", "scalar", "tensor", "gpsimd")
    QSEMS = {"sync": ("sWa", "sWb"), "gpsimd": ("sGa", "sGb")}

    def __init__(self, nc, ctx, semnames):
        self.nc = nc
        self.ops = {e: [] for e in self.ENGINES}
        self.sem = {}
        self.cnt = {}
        for n in semnames:
            self.sem[n] = ctx.enter_context(nc.semaphore(n))
            self.cnt[n] = 0
        self.qn = {"sync": 0, "gpsimd": 0}
        self.confirmed = {}
        self.defq = {"scalar": [], "sync": []}

    def wait(self, eng, name, val=None, defer=False):
        v = self.cnt[name] if val is None else val
        if v <= 0:
            return
        sem = self.sem[name]
        f = lambda e, _s=sem, _v=v: e.wait_ge(_s, _v)
        if defer:
            self.defq[eng].append(f)
        else:
            self.ops[eng].append(f)
            if eng in self.QSEMS and name in self.QSEMS[eng]:
                k = (eng, name)
                self.confirmed[k] = max(self.confirmed.get(k, 0), v)

    def waittok(self, eng, tok, defer=False):
        if tok is not None:
            self.wait(eng, tok[0], tok[1], defer=defer)

    def qwait(self, eng, queue):
        for s in self.QSEMS[queue]:
            self.wait(eng, s)

    def op(self, eng, builder, inc=None, n=1, defer=False):
        lst = self.defq[eng] if defer else self.ops[eng]
        if inc is None:
            lst.append(builder)
            return None
        sem = self.sem[inc]
        self.cnt[inc] += n
        lst.append(lambda e, _b=builder, _s=sem, _n=n: _b(e).then_inc(_s, _n))
        return self.cnt[inc]

    def edge(self, eng, builder, sem, defer=False):
        m = self.op(eng, builder, inc=sem, defer=defer)
        self.wait(eng, sem, m, defer=defer)
        return m

    def pump(self, fracs):
        for eng, n in fracs.items():
            take, self.defq[eng] = self.defq[eng][:n], self.defq[eng][n:]
            self.ops[eng].extend(take)

    def pump_all(self):
        for eng in list(self.defq):
            self.ops[eng].extend(self.defq[eng])
            self.defq[eng] = []

    def dma(self, eng, dst, src):
        sems = self.QSEMS[eng]
        s = sems[self.qn[eng] % 2]
        self.qn[eng] += 1
        k = (eng, s)
        pre = self.cnt[s]
        if pre > self.confirmed.get(k, 0):
            self.wait(eng, s, pre)
        val = self.op(eng, lambda e, _d=dst, _s=src: e.dma_start(_d, _s), inc=s, n=16)
        return (s, val)


def build_core_program(cfg, dbg=False, dbgsel=7, upto="ffn"):
    c = cfg
    nc = bass.Bass("TRN2", target_bir_lowering=False, detect_race_conditions=False)

    X = nc.dram_tensor("X", [c.ROWS, c.D], F32, kind="ExternalInput")
    WGT = nc.dram_tensor("WGT", [c.D, c.I], F32, kind="ExternalInput")
    WUT = nc.dram_tensor("WUT", [c.D, c.I], F32, kind="ExternalInput")
    WDT = nc.dram_tensor("WDT", [c.I, c.D], F32, kind="ExternalInput")
    WT = nc.dram_tensor("WT", [c.D, c.D], F32, kind="ExternalInput")
    BIAS = nc.dram_tensor("BIAS", [1, c.D], F32, kind="ExternalInput")
    ATT = nc.dram_tensor("ATT", [c.D, c.ER], F32, kind="ExternalInput")
    MTT = nc.dram_tensor("MTT", [c.ER, c.D], F32, kind="ExternalInput")
    PR = nc.dram_tensor("PR", [16], F32, kind="ExternalInput")
    IDT = nc.dram_tensor("IDT", [128, 128], BF16, kind="ExternalInput")
    OUT = nc.dram_tensor("OUT", [c.ROWS, c.D], F32, kind="ExternalOutput")

    ikind = "ExternalOutput" if dbg else "Internal"
    sD = nc.dram_tensor("sD", [c.ROWS, c.I], BF16, kind=ikind)
    qD = nc.dram_tensor("qD", [c.ROWS, c.I], FP16, kind=ikind)
    uD = nc.dram_tensor("uD", [c.ROWS, c.I], BF16, kind=ikind)
    x1D = nc.dram_tensor("x1D", [c.ROWS, c.D], F32, kind=ikind)

    NSLOT = c.HC + 1
    HW = (c.QNIMX // 2 + c.QNIMX % 2) * 128
    big1_b = max(c.HC * c.ROWS * 2,
                 c.QNIMX * c.ROWS * 2 + 4 * c.QNIMX * 128 * 2 + 2 * 2048 * 4)
    tail_b = c.I * 2 + 2 * c.QW * 2
    up_b = NSLOT * 1024 + 4 * 2048 + 4 * 1024 + 2 * 2048 + tail_b
    ln_b = 2 * c.D * 4 + c.D * 2
    dn_b = 2 * c.QNIMX * 512 * 2 + 2 * 2048 + tail_b
    ffn_b = NSLOT * 1024 + 4 * 2048 + 4 * 1024 + c.NT * c.NE * 128 * 2 \
        + 8 * c.ER * 2 + 4 * 1024 + 2 * 512 * 4 + 2 * 512 * 4 + c.D * 2
    a2_b = max(up_b, ln_b, dn_b, ffn_b)
    tail0 = a2_b - tail_b

    with ExitStack() as ctx:
        e_ = ctx.enter_context
        A1 = e_(nc.sbuf_tensor("A1", [128, big1_b], U8))
        A2 = e_(nc.sbuf_tensor("A2", [128, a2_b], U8))

        def carve(arena, off, dt, shape):
            nb = {F32: 4, BF16: 2, I16: 2, FP16: 2}[dt] * int(np.prod(shape))
            assert off + nb <= arena.shape[-1], (off, nb, arena.shape)
            ap = arena[:, off: off + nb].bitcast(dt)
            if len(shape) == 1:
                return ap
            letters = "abcd"[: len(shape)]
            spec = "p (" + " ".join(letters) + ") -> p " + " ".join(letters)
            return ap.rearrange(spec, **{letters[i]: shape[i] for i in range(len(shape) - 1)})

        yT = carve(A1, 0, BF16, [c.HC, c.ROWS])
        mTq = carve(A1, 0, BF16, [c.QNIMX, c.ROWS])
        o1 = c.QNIMX * c.ROWS * 2
        FW = c.QNIMX * 128
        pr_s = carve(A1, o1, BF16, [FW]); o1 += FW * 2
        pr_q = carve(A1, o1, FP16, [FW]); o1 += FW * 2
        pr_u = carve(A1, o1, BF16, [FW]); o1 += FW * 2
        pr_b = carve(A1, o1, BF16, [FW]); o1 += FW * 2
        x1stg = carve(A1, o1, F32, [2, 512]); o1 += 2 * 512 * 4
        xres = carve(A1, o1, F32, [2, 512]); o1 += 2 * 512 * 4
        pnl = carve(A2, 0, BF16, [NSLOT, 512])
        rs = carve(A2, NSLOT * 1024, F32, [4, 512])
        o2 = NSLOT * 1024 + 4 * 2048
        qt = carve(A2, tail0, FP16, [c.I])
        junkD = carve(A2, tail0 + c.I * 2, BF16, [c.QW])
        junkA = carve(A2, tail0 + c.I * 2 + c.QW * 2, BF16, [c.QW])
        sstg = carve(A2, o2, BF16, [4, 512])
        qsv = carve(A2, o2, FP16, [4, 512])
        sf32 = carve(A2, o2 + 8 * 1024, F32, [2, 512])
        xst = carve(A2, 0, F32, [2, c.D])
        ystg = carve(A2, tail0, BF16, [c.D])
        slab = carve(A2, 0, BF16, [2, c.QNIMX, 512])
        rsd = carve(A2, 2 * c.QNIMX * 512 * 2, F32, [2, 512])
        of = NSLOT * 1024 + 4 * 2048
        atr = carve(A2, of, BF16, [4, 512]); of += 4 * 1024
        hTb = carve(A2, of, BF16, [c.NT, c.NE, 128]); of += c.NT * c.NE * 128 * 2
        hall = carve(A2, of, BF16, [8, c.ER]); of += 8 * c.ER * 2
        mrng = carve(A2, of, BF16, [4, 512]); of += 4 * 1024
        ostg = carve(A2, of, F32, [2, 512]); of += 2 * 512 * 4
        x1res = carve(A2, of, F32, [2, 512]); of += 2 * 512 * 4
        bbf = A2[0:1, of: of + c.D * 2].bitcast(BF16)

        stats = e_(nc.sbuf_tensor("stats", [128, c.D // 512, 6], F32))
        lnMV = e_(nc.sbuf_tensor("lnMV", [128, c.NT, 2], F32))
        lnW = e_(nc.sbuf_tensor("lnW", [128, 5, c.NT], F32))
        rstd = e_(nc.sbuf_tensor("rstd", [128, c.NT], F32))
        thr = e_(nc.sbuf_tensor("thr", [128, c.NT], F32))
        bst = e_(nc.sbuf_tensor("bst", [128, 32], F32))
        pi16 = e_(nc.sbuf_tensor("pi16", [128, 4], I16))
        ones = e_(nc.sbuf_tensor("ones", [1, 128], BF16))
        pv = e_(nc.sbuf_tensor("pv", [128, 1], F32))
        idt = e_(nc.sbuf_tensor("idt", [128, 128], BF16))
        cst = e_(nc.sbuf_tensor("cst", [128, 8], F32))
        ps = e_(nc.psum_tensor("ps", [128, 8, 512], F32))

        psTR = [ps[:, 6, :].bitcast(BF16).rearrange("p (s f) -> p s f", s=8),
                ps[:, 7, :].bitcast(BF16).rearrange("p (s f) -> p s f", s=8)]

        yTdbg = nc.dram_tensor("yTdbg", [128, c.HC * c.ROWS], BF16,
                               kind="ExternalOutput") if dbg else None
        thrdbg = nc.dram_tensor("thrdbg", [128, c.NT], F32,
                                kind="ExternalOutput") if dbg else None
        x1q0 = nc.dram_tensor("x1q0", [4, c.ROWS, c.D], F32,
                              kind="ExternalOutput") if dbg else None
        mtdbg = nc.dram_tensor("mtdbg", [128, 4 * c.QNIMX * c.ROWS], BF16,
                               kind="ExternalOutput") if dbg else None

        prog = Prog(nc, ctx, ["sWa", "sWb", "sGa", "sGb", "sQt",
                              "cV", "cA", "cT", "cM", "cP", "cB", "cBa", "cBd"])
        P = prog

        B0 = 16384.0
        T0 = (np.uint16(16384).view(np.float16).astype(np.float64)
              + np.uint16(16383).view(np.float16).astype(np.float64)) / 2.0
        PBIAS = -(c.KEEP - c.NHALF / 2.0 - 0.5)

        P.dma("gpsimd", idt[:], IDT.ap())
        for k in range(8):
            P.dma("gpsimd", pv[16 * k:16 * (k + 1), 0:1], PR.ap()[0:16])
        P.qwait("vector", "gpsimd")
        P.op("vector", lambda e: e.memset(ones[:], 1.0))
        P.op("vector", lambda e: e.memset(cst[:, 0:1], B0))
        P.op("vector", lambda e: e.memset(cst[:, 1:2], PBIAS))
        P.op("vector", lambda e: e.memset(cst[:, 2:3], float(T0)))
        P.op("vector", lambda e: e.memset(cst[:, 3:4], float(-T0)), inc="cV")
        prologue_cV = P.cnt["cV"]
        P.wait("scalar", "cV", prologue_cV)
        P.qwait("tensor", "gpsimd")
        P.wait("tensor", "cV", prologue_cV)

        trst = {"grp": 0, "marks": []}

        def transpose_blocks(srcs, dst_ap, evac="scalar", evac_op=None,
                             evac_sem="cA", marks_back=2):
            g = trst["grp"]
            bank = g % 2
            base = 4 * ((g // 2) % 2)
            if len(trst["marks"]) >= marks_back:
                ms, mv = trst["marks"][-marks_back]
                P.wait("tensor", ms, mv)
            nb = len(srcs)
            tm = None
            for bi in range(4):
                src = srcs[bi] if bi < nb else idt[:]
                tm = P.op("tensor",
                          lambda e, _s=src, _d=psTR[bank][:, base + bi, :]:
                          e.transpose(_d, _s, idt[:]),
                          inc=("cM" if bi == 3 else None))
            P.wait(evac, "cM", tm)
            if evac_op is None:
                mark = P.op(evac,
                            lambda e, _d=dst_ap, _sl=psTR[bank][:, base:base + nb, :]:
                            e.copy(_d, _sl), inc=evac_sem)
            else:
                mark = evac_op(psTR[bank][:, base:base + nb, :], dst_ap)
            trst["grp"] += 1
            trst["marks"].append((evac_sem, mark))
            return mark, tm

        xst_guard = [None, None]
        ystg_guard = [None]

        def emit_ln(src_dram, after_down=False):
            P.wait("gpsimd", "cT")
            P.wait("gpsimd", "cM")
            P.wait("gpsimd", "cV")
            P.wait("gpsimd", "cA")
            if after_down:
                P.qwait("gpsimd", "gpsimd")
            for ch in range(c.NT):
                buf = ch % 2
                if xst_guard[buf] is not None:
                    P.wait("gpsimd", "cV", xst_guard[buf])
                tok = P.dma("gpsimd", xst[:, buf, :],
                            src_dram.ap()[128 * ch: 128 * (ch + 1), :])
                P.waittok("vector", tok)
                P.wait("vector", "cV")
                for u in range(c.D // 512):
                    P.op("vector", lambda e, _u=u, _b=buf: e.bn_stats(
                        stats[:, _u, :], xst[:, _b, 512 * _u:512 * (_u + 1)]),
                        inc=("cV" if u == c.D // 512 - 1 else None))
                P.wait("vector", "cV")
                xst_guard[buf] = P.op("vector", lambda e, _ch=ch: e.bn_aggr(
                    lnMV[:, _ch, :], stats[:].rearrange("p a b -> p (a b)")), inc="cV")
            w_, r_, t1, t2, t3 = (lnW[:, k, :] for k in range(5))
            P.wait("vector", "cV")
            P.edge("vector", lambda e: e.tensor_scalar(w_, lnMV[:, :, 1], LN_EPS, None, ALU.add), "cV")
            P.edge("vector", lambda e: e.memset(r_, 0.05), "cV")
            for _ in range(18):
                P.edge("vector", lambda e: e.tensor_mul(t1, r_, r_), "cV")
                P.edge("vector", lambda e: e.tensor_mul(t2, t1, w_), "cV")
                P.edge("vector", lambda e: e.tensor_scalar(t3, t2, -0.5, 1.5, ALU.mult, ALU.add), "cV")
                P.edge("vector", lambda e: e.tensor_mul(r_, r_, t3), "cV")
            rstd_mark = P.edge("vector", lambda e: e.tensor_copy(rstd[:], r_), "cV")
            for ch in range(c.NT):
                buf = ch % 2
                if xst_guard[buf] is not None:
                    P.wait("gpsimd", "cV", xst_guard[buf])
                tok = P.dma("gpsimd", xst[:, buf, :],
                            src_dram.ap()[128 * ch: 128 * (ch + 1), :])
                P.waittok("vector", tok)
                P.wait("vector", "cV", rstd_mark)
                if ystg_guard[0] is not None:
                    P.wait("vector", "cM", ystg_guard[0])
                xst_guard[buf] = P.op("vector", lambda e, _ch=ch, _b=buf: e.tensor_scalar(
                    ystg[:], xst[:, _b, :], lnMV[:, _ch, 0:1], rstd[:, _ch:_ch + 1],
                    ALU.subtract, ALU.mult), inc="cV")
                P.wait("tensor", "cV", xst_guard[buf])
                for h0 in range(0, c.HC, 4):
                    nb = min(4, c.HC - h0)
                    srcs = [ystg[:, 128 * (h0 + bi):128 * (h0 + bi + 1)] for bi in range(nb)]
                    _, tm = transpose_blocks(srcs, yT[:, h0:h0 + nb, 128 * ch:128 * (ch + 1)])
                ystg_guard[0] = P.cnt["cM"]
            lndr = P.op("scalar", lambda e: e.drain(), inc="cA")
            P.wait("tensor", "cA", lndr)

        bis_done = {}
        qt_toks = {}

        def build_bisect_all(gate_snap):
            cn = [bst[:, k:k + 1] for k in range(4)]
            cntD1 = bst[:, 4:5]
            ta_, tb_ = bst[:, 5:6], bst[:, 6:7]
            aa = [bst[:, 8 + k:9 + k] for k in range(4)]
            tsA = bst[:, 12:13]
            t1c = bst[:, 13:14]
            pred = bst[:, 14:15]
            Bp = [bst[:, 16:17], bst[:, 17:18]]
            Bst = bst[:, 18:19]
            pairB = bst[:, 20:22]
            vpair = bst[:, 24:26]
            tp = bst[:, 26:27]
            thrS = bst[:, 27:28]
            thrSn = bst[:, 28:29]

            def t_chain(src_cell, dst_s, dst_sn):
                P.edge("vector", lambda e: e.tensor_copy(pairB[:, 0:1], src_cell), "cBd")
                P.edge("vector", lambda e: e.tensor_scalar(
                    pairB[:, 1:2], src_cell, -1.0, 0.0, ALU.add, ALU.max), "cBd")
                P.edge("vector", lambda e: e.tensor_copy(pi16[:, 0:2], pairB), "cBd")
                P.edge("vector", lambda e: e.tensor_copy(
                    vpair, pi16[:, 0:2].bitcast(FP16)), "cBd")
                P.edge("vector", lambda e: e.tensor_add(tp, vpair[:, 0:1], vpair[:, 1:2]), "cBd")
                P.edge("vector", lambda e, _d=dst_s: e.tensor_scalar(
                    _d, tp, 0.5, None, ALU.mult), "cBd")
                return P.edge("vector", lambda e, _d=dst_sn: e.tensor_scalar(
                    _d, tp, -0.5, None, ALU.mult), "cBd")

            upd = None
            chain_mark = [None]
            for tile in range(c.NT):
                if tile == 0:
                    for s_, v_ in gate_snap:
                        P.wait("scalar", s_, v_, defer=True)
                P.cnt["sQt"] += 16
                qt_toks[tile] = P.cnt["sQt"]
                P.op("scalar", lambda e, _t=tile: e.dma_start(
                    qt[:], qD.ap()[128 * _t:128 * (_t + 1), :]).then_inc(P.sem["sQt"], 16),
                    defer=True)
                P.wait("vector", "sQt", qt_toks[tile])
                P.wait("scalar", "sQt", qt_toks[tile], defer=True)
                for it in range(c.N_BIS):
                    sS = thrS if it > 0 else cst[:, 2:3]
                    sSn = thrSn if it > 0 else cst[:, 3:4]
                    Bc = cst[:, 0:1] if it == 0 else Bp[(it - 1) % 2]
                    Bn = Bp[it % 2]
                    step = float(8192 >> it) if it < 14 else 0.5
                    for qq in range(4):
                        P.op("vector", lambda e, _q=qq, _s=sS: e.tensor_scalar(
                            junkD[:], qt[:, c.QW * _q:c.QW * (_q + 1)], _s, 0.0,
                            ALU.is_ge, ALU.add, accum_out=cn[_q]),
                            inc=("cBd" if qq == 3 else None))
                    P.wait("vector", "cBd")
                    P.edge("vector", lambda e: e.tensor_add(ta_, cn[0], cn[1]), "cBd")
                    P.edge("vector", lambda e: e.tensor_add(tb_, cn[2], cn[3]), "cBd")
                    dve_cnt = P.edge("vector", lambda e: e.tensor_add(cntD1, ta_, tb_), "cBd")
                    if it > 0:
                        P.wait("scalar", "cBd", chain_mark[0], defer=True)
                    for qq in range(4):
                        P.op("scalar", lambda e, _q=qq, _s=sSn: e.activation(
                            junkA[:], qt[:, c.NHALF + c.QW * _q:c.NHALF + c.QW * (_q + 1)],
                            AF.Sign, bias=_s, scale=1.0, accum_out=aa[_q]),
                            inc=("cBa" if qq == 3 else None), defer=True)
                    P.wait("scalar", "cBa", defer=True)
                    P.edge("scalar", lambda e: e.activation(
                        tsA, aa[1], AF.Identity, bias=aa[0], scale=1.0), "cBa", defer=True)
                    P.edge("scalar", lambda e: e.activation(
                        tsA, aa[2], AF.Identity, bias=tsA, scale=1.0), "cBa", defer=True)
                    P.edge("scalar", lambda e: e.activation(
                        tsA, aa[3], AF.Identity, bias=tsA, scale=1.0), "cBa", defer=True)
                    P.wait("scalar", "cBd", dve_cnt, defer=True)
                    P.edge("scalar", lambda e: e.activation(
                        t1c, tsA, AF.Identity, bias=cntD1, scale=0.5), "cBa", defer=True)
                    P.edge("scalar", lambda e: e.activation(
                        pred, t1c, AF.Sign, bias=cst[:, 1:2], scale=1.0), "cBa", defer=True)
                    upd = P.edge("scalar", lambda e, _bc=Bc, _bn=Bn, _st=step: e.activation(
                        _bn, pred, AF.Identity, bias=_bc, scale=_st), "cBa", defer=True)
                    if it < c.N_BIS - 1:
                        P.wait("vector", "cBa", upd)
                        chain_mark[0] = t_chain(Bn, thrS, thrSn)
                P.wait("vector", "cBa", upd)
                P.edge("vector", lambda e: e.tensor_scalar(
                    Bst, Bp[(c.N_BIS - 1) % 2], -0.5, None, ALU.add), "cBd")
                t_chain(Bst, thr[:, tile:tile + 1], thrSn)
                bis_done[tile] = P.op(
                    "vector", lambda e, _t=tile: e.tensor_copy(
                        bst[:, 30:31], thr[:, _t:_t + 1]), inc="cB")
                P.wait("vector", "cB", bis_done[tile])

        chase = {"cast_mark": {}, "slot_mark": {}, "ring_tok": [None] * 4}

        def stream_chunk(qidx, src_ap, jw):
            rslot = qidx % 4
            if chase["ring_tok"][rslot] is not None:
                P.wait("sync", "cP", chase["ring_tok"][rslot])
            tok = P.dma("sync", rs[:, rslot, 0:jw], src_ap)
            P.waittok("gpsimd", tok)
            lcm = chase["slot_mark"].get(qidx - NSLOT)
            if lcm is not None:
                P.wait("gpsimd", "cT", lcm)
            m = P.op("gpsimd", lambda e, _r=rslot, _s=qidx % NSLOT: e.tensor_copy(
                pnl[:, _s, :], rs[:, _r, :]), inc="cP")
            chase["cast_mark"][qidx] = m
            chase["ring_tok"][rslot] = m
            return m

        qbase = [0]

        def emit_proj(w_dram, kind, pump_fr=None, fresh=True):
            P.wait("tensor", "cA")
            P.wait("tensor", "cV")
            P.qwait("scalar", "gpsimd")
            if fresh:
                P.wait("gpsimd", "cV")
                P.wait("gpsimd", "cM")
                P.wait("gpsimd", "cA")
            nj = len(c.JB)
            evac_w = {}
            evac_mark = {}
            stg_tok = {}
            grp = [0]
            ct0 = P.cnt["cT"]

            def emit_casts(j):
                j0, jw = c.JB[j]
                for h in range(c.HC):
                    q = j * c.HC + h
                    chase["slot_mark"][qbase[0] + q] = ct0 + q + 1
                    stream_chunk(qbase[0] + q,
                                 w_dram.ap()[128 * h:128 * (h + 1), j0:j0 + jw], jw)

            def emit_spill(j, t):
                j0, jw = c.JB[j]
                sl = (j * c.NT + t) % 2
                P.wait("gpsimd", "cA", evac_mark[(j, t)])
                tk = P.dma("gpsimd", (sD if kind == "gate" else uD)
                           .ap()[128 * t:128 * (t + 1), j0:j0 + jw], sstg[:, sl, 0:jw])
                if kind == "gate":
                    tk2 = P.dma("gpsimd", qD.ap()[128 * t:128 * (t + 1), j0:j0 + jw],
                                qsv[:, 2 + sl, 0:jw])
                    stg_tok[(j, t)] = (tk, tk2)
                else:
                    stg_tok[(j, t)] = (tk,)

            emit_casts(0)
            for j in range(nj):
                j0, jw = c.JB[j]
                for t in range(c.NT):
                    bank = grp[0] % 4
                    if grp[0] >= 4:
                        P.wait("tensor", "cA", evac_mark[divmod(grp[0] - 4, c.NT)])
                    gmark = None
                    for h in range(c.HC):
                        qidx = qbase[0] + j * c.HC + h
                        if t == 0:
                            P.wait("tensor", "cP", chase["cast_mark"][qidx])
                        inc = None
                        if t == c.NT - 1:
                            inc = "cT"
                        elif h == c.HC - 1:
                            inc = "cM"
                        gmark = P.op("tensor", lambda e, _h=h, _t=t, _q=qidx, _b=bank, _jw=jw:
                                     e.matmul(ps[:, _b, 0:_jw],
                                              yT[:, _h, 128 * _t:128 * (_t + 1)],
                                              pnl[:, _q % NSLOT, 0:_jw],
                                              start=(_h == 0), stop=(_h == c.HC - 1)),
                                     inc=inc)
                    evac_w[(j, t)] = ("cT" if t == c.NT - 1 else "cM", gmark)
                    grp[0] += 1
                    sl = (j * c.NT + t) % 2
                    if (j * c.NT + t) >= 2:
                        for tk in stg_tok[divmod(j * c.NT + t - 2, c.NT)]:
                            P.waittok("scalar", tk)
                    P.wait("scalar", *evac_w[(j, t)])
                    if kind == "gate":
                        P.op("scalar", lambda e, _b=bank, _sl=sl, _jw=jw: e.activation(
                            sf32[:, _sl, 0:_jw], ps[:, _b, 0:_jw], AF.Silu))
                        P.op("scalar", lambda e, _sl=sl, _jw=jw: e.copy(
                            sstg[:, _sl, 0:_jw], sf32[:, _sl, 0:_jw]))
                        evac_mark[(j, t)] = P.op(
                            "scalar", lambda e, _sl=sl, _jw=jw: e.activation(
                                qsv[:, 2 + _sl, 0:_jw], sf32[:, _sl, 0:_jw], AF.Abs),
                            inc="cA")
                    else:
                        evac_mark[(j, t)] = P.op(
                            "scalar", lambda e, _b=bank, _sl=sl, _jw=jw: e.copy(
                                sstg[:, _sl, 0:_jw], ps[:, _b, 0:_jw]), inc="cA")
                    emit_spill(j, t)
                if j + 1 < nj:
                    emit_casts(j + 1)
                if pump_fr:
                    P.pump(pump_fr)
            qbase[0] += nj * c.HC
            if pump_fr:
                P.pump_all()

        def emit_down(mtdbg=None, x1q0=None):
            P.wait("tensor", "cA")
            P.wait("tensor", "cV")
            P.wait("sync", "cT")
            P.wait("sync", "cP")
            P.wait("gpsimd", "cT")
            P.wait("gpsimd", "cA")
            P.wait("gpsimd", "cV")
            P.qwait("gpsimd", "gpsimd")
            x1tok = {}
            sgrp = [0]
            dn_evac = {}
            slab_cons = {}
            rsd_tok = {}
            pr_guard = {}
            prb_guard = [None]

            for qtr in range(4):
                vdr = P.op("vector", lambda e: e.drain(), inc="cV")
                adr = P.op("scalar", lambda e: e.drain(), inc="cA")
                P.wait("tensor", "cV", vdr)
                P.wait("tensor", "cA", adr)
                P.wait("gpsimd", "cT")
                P.wait("gpsimd", "cM")
                P.wait("gpsimd", "cV")
                P.wait("gpsimd", "cA")
                P.wait("sync", "cT")
                P.wait("vector", "cT")
                P.wait("vector", "cM")
                P.wait("scalar", "cT")
                P.wait("tensor", "cV")
                P.wait("tensor", "cA")
                P.wait("tensor", "cP")
                qc0, qn = c.QS[qtr]
                g0 = qc0 * 128
                for tile in range(c.NT):
                    gw = qn * 128
                    if pr_guard.get(0) is not None:
                        P.wait("gpsimd", "cV", pr_guard[0])
                    t1 = P.dma("gpsimd", pr_s[:, 0:gw],
                               sD.ap()[128 * tile:128 * (tile + 1), g0:g0 + gw])
                    t2 = P.dma("gpsimd", pr_q[:, 0:gw],
                               qD.ap()[128 * tile:128 * (tile + 1), g0:g0 + gw])
                    t3 = P.dma("gpsimd", pr_u[:, 0:gw],
                               uD.ap()[128 * tile:128 * (tile + 1), g0:g0 + gw])
                    for tk in (t1, t2, t3):
                        P.waittok("vector", tk)
                    P.wait("vector", "cB", bis_done[tile])
                    if prb_guard[0] is not None:
                        P.wait("vector", "cM", prb_guard[0])
                    P.op("vector", lambda e, _t=tile, _w=gw: e.tensor_scalar(
                        pr_b[:, 0:_w], pr_q[:, 0:_w], thr[:, _t:_t + 1],
                        None, ALU.is_ge))
                    P.op("vector", lambda e, _w=gw: e.tensor_mul(
                        pr_q[:, 0:_w], pr_b[:, 0:_w], pr_s[:, 0:_w]))
                    pr_guard[0] = P.edge("vector", lambda e, _w=gw: e.tensor_mul(
                        pr_b[:, 0:_w], pr_q[:, 0:_w], pr_u[:, 0:_w]), "cV")
                    aged = P.op("vector", lambda e: e.drain(), inc="cV")
                    P.wait("tensor", "cV", aged)
                    tmlast = None
                    for b0 in range(0, qn, 4):
                        nb = min(4, qn - b0)
                        srcs = [pr_b[:, 128 * (b0 + bi):128 * (b0 + bi + 1)]
                                for bi in range(nb)]
                        _, tmlast = transpose_blocks(
                            srcs, mTq[:, b0:b0 + nb,
                                      128 * tile:128 * (tile + 1)])
                    prb_guard[0] = tmlast
                qdr = P.op("scalar", lambda e: e.drain(), inc="cA")
                P.wait("tensor", "cA", qdr)
                if mtdbg is not None:
                    P.wait("gpsimd", "cA")
                    P.wait("gpsimd", "cV")
                    _dtok = P.dma("gpsimd", mtdbg.ap()[:, qtr * c.QNIMX * c.ROWS:
                                                       (qtr + 1) * c.QNIMX * c.ROWS],
                                  mTq[:].rearrange("p a b -> p (a b)"))
                    P.waittok("scalar", _dtok)
                    P.waittok("vector", _dtok)
                    P.waittok("tensor", _dtok)
                for n in range(c.D // 512):
                    gidx = qtr * (c.D // 512) + n
                    nb = gidx % 2
                    for i in range(qn):
                        rslot = i % 2
                        if rsd_tok.get(rslot) is not None:
                            P.wait("sync", "cP", rsd_tok[rslot])
                        tok = P.dma("sync", rsd[:, rslot, :],
                                    WDT.ap()[128 * (qc0 + i):128 * (qc0 + i + 1),
                                             512 * n:512 * (n + 1)])
                        P.waittok("gpsimd", tok)
                        lcm = slab_cons.get((gidx - 2, i))
                        if lcm is not None:
                            P.wait("gpsimd", "cT", lcm)
                        m = P.op("gpsimd", lambda e, _nb=nb, _i=i, _r=rslot: e.tensor_copy(
                            slab[:, _nb, _i, :], rsd[:, _r, :]), inc="cP")
                        rsd_tok[rslot] = m
                        slab_cons[(gidx, "p", i)] = m
                    for rt in range(c.NT):
                        bank = sgrp[0] % 6
                        if sgrp[0] >= 6:
                            P.wait("tensor", "cV", dn_evac[sgrp[0] - 6])
                        gmark = None
                        for i in range(qn):
                            if rt == 0:
                                P.wait("tensor", "cP", slab_cons[(gidx, "p", i)])
                            inc = None
                            if rt == c.NT - 1:
                                inc = "cT"
                            elif i == qn - 1:
                                inc = "cM"
                            gmark = P.op("tensor", lambda e, _i=i, _rt=rt, _b=bank, _nb=nb,
                                         _qn=qn:
                                         e.matmul(ps[:, _b, :],
                                                  mTq[:, _i, 128 * _rt:128 * (_rt + 1)],
                                                  slab[:, _nb, _i, :],
                                                  start=(_i == 0), stop=(_i == _qn - 1)),
                                         inc=inc)
                            if rt == c.NT - 1:
                                slab_cons[(gidx, i)] = gmark
                        ew = ("cT" if rt == c.NT - 1 else "cM", gmark)
                        xb = sgrp[0] % 2
                        if sgrp[0] >= 2:
                            P.wait("gpsimd", "cV", dn_evac[sgrp[0] - 2])
                        if qtr == 0:
                            tok = P.dma("gpsimd", xres[:, xb, :],
                                        X.ap()[128 * rt:128 * (rt + 1), 512 * n:512 * (n + 1)])
                        else:
                            P.waittok("gpsimd", x1tok[(n, rt)])
                            tok = P.dma("gpsimd", xres[:, xb, :],
                                        x1D.ap()[128 * rt:128 * (rt + 1), 512 * n:512 * (n + 1)])
                        P.waittok("vector", tok)
                        P.wait("vector", *ew)
                        if sgrp[0] >= 2:
                            P.waittok("vector", x1tok[("w", sgrp[0] - 2)])
                        dn_evac[sgrp[0]] = P.op(
                            "vector", lambda e, _b=bank, _xb=xb: e.tensor_add(
                                x1stg[:, _xb, :], ps[:, _b, :], xres[:, _xb, :]), inc="cV")
                        P.wait("gpsimd", "cV", dn_evac[sgrp[0]])
                        wtok = P.dma("gpsimd",
                                     x1D.ap()[128 * rt:128 * (rt + 1), 512 * n:512 * (n + 1)],
                                     x1stg[:, xb, :])
                        x1tok[(n, rt)] = wtok
                        x1tok[("w", sgrp[0])] = wtok
                        sgrp[0] += 1
                if x1q0 is not None:
                    P.qwait("gpsimd", "gpsimd")
                    for rtd in range(c.NT):
                        _dt = P.dma("gpsimd",
                                    x1q0.ap()[qtr, 128 * rtd:128 * (rtd + 1), :],
                                    x1D.ap()[128 * rtd:128 * (rtd + 1), :])
                    P.waittok("sync", _dt)
                    P.waittok("scalar", _dt)
                    P.waittok("vector", _dt)
                    P.waittok("tensor", _dt)

        def emit_ffn():
            P.wait("tensor", "cA")
            P.wait("tensor", "cV")
            P.wait("sync", "cT")
            P.wait("sync", "cP")
            P.wait("gpsimd", "cT")
            P.wait("gpsimd", "cM")
            P.wait("gpsimd", "cA")
            P.wait("gpsimd", "cV")
            at_mark = {}
            at_cons = {}
            tr_mark = {}

            def pv_evac_mk(t):
                def pv_evac(psl, dst):
                    return P.op("vector", lambda e, _p=psl, _d=dst: e.tensor_scalar(
                        _d, _p, pv[:, 0:1], None, ALU.mult), inc="cV")
                return pv_evac

            nhv = (c.NT + 7) // 8
            for half in range(nhv):
                t0h = half * 8
                ntl = min(8, c.NT - t0h)
                if half > 0:
                    P.wait("tensor", "cA")
                for h in range(c.HC):
                    ai = half * c.HC + h
                    rslot = ai % 4
                    if chase["ring_tok"][rslot] is not None:
                        P.wait("sync", "cP", chase["ring_tok"][rslot])
                    tok = P.dma("sync", rs[:, rslot, 0:c.ER],
                                ATT.ap()[128 * h:128 * (h + 1), :])
                    P.waittok("gpsimd", tok)
                    if at_cons.get(ai - 4) is not None:
                        P.wait("gpsimd", "cT", at_cons[ai - 4])
                    at_mark[ai] = P.op("gpsimd", lambda e, _a=ai, _r=rslot: e.tensor_copy(
                        atr[:, _a % 4, :], rs[:, _r, :]), inc="cP")
                    chase["ring_tok"][rslot] = at_mark[ai]
                    for tl in range(ntl):
                        t = t0h + tl
                        if tl == 0:
                            P.wait("tensor", "cP", at_mark[ai])
                        m = P.op("tensor", lambda e, _h=h, _t=t, _b=tl, _a=ai: e.matmul(
                            ps[:, _b, 0:c.ER],
                            yT[:, _h, 128 * _t:128 * (_t + 1)], atr[:, _a % 4, 0:c.ER],
                            start=(_h == 0), stop=(_h == c.HC - 1)),
                            inc=("cT" if tl == ntl - 1 else None))
                    at_cons[ai] = m
                P.wait("scalar", "cT")
                hms = {}
                for tl in range(ntl):
                    t = t0h + tl
                    if tr_mark.get(t - 8) is not None:
                        P.wait("scalar", "cM", tr_mark[t - 8])
                    hms[t] = P.op("scalar", lambda e, _b=tl, _t=t: e.copy(
                        hall[:, _t % 8, :], ps[:, _b, 0:c.ER]), inc="cA")
                hage = P.op("scalar", lambda e: e.drain(), inc="cA")
                for tl in range(ntl):
                    t = t0h + tl
                    P.wait("tensor", "cA", hage)
                    srcs = [hall[:, t % 8, 128 * eb:128 * (eb + 1)] for eb in range(c.NE)]
                    _, tm = transpose_blocks(srcs, hTb[:, t, :, :],
                                             evac="vector", evac_op=pv_evac_mk(t),
                                             evac_sem="cV", marks_back=1)
                    tr_mark[t] = tm
            hdr = P.op("vector", lambda e: e.drain(), inc="cV")
            P.wait("tensor", "cV", hdr)
            bm = None
            for k in range(c.D // 512):
                rslot = k % 4
                if chase["ring_tok"][rslot] is not None:
                    P.wait("sync", "cP", chase["ring_tok"][rslot])
                tok = P.dma("sync", rs[0:1, rslot, :], BIAS.ap()[0:1, 512 * k:512 * (k + 1)])
                P.waittok("gpsimd", tok)
                bm = P.op("gpsimd", lambda e, _k=k, _r=rslot: e.tensor_copy(
                    bbf[:, 512 * _k:512 * (_k + 1)], rs[0:1, _r, :]), inc="cP")
                chase["ring_tok"][rslot] = bm
            P.wait("tensor", "cP", bm)
            evac_mark = {}
            out_tok = {}
            m_mark = {}
            m_cons = {}
            grp = [0]
            nj = len(c.WJB)

            def emit_wcasts(j):
                j0, jw = c.WJB[j]
                for h in range(c.HC):
                    stream_chunk(qbase[0] + j * c.HC + h,
                                 WT.ap()[128 * h:128 * (h + 1), j0:j0 + jw], jw)
                for ec in range(c.NE):
                    mi = j * c.NE + ec
                    rslot = mi % 4
                    if chase["ring_tok"][rslot] is not None:
                        P.wait("sync", "cP", chase["ring_tok"][rslot])
                    tok = P.dma("sync", rs[:, rslot, :],
                                MTT.ap()[128 * ec:128 * (ec + 1), j0:j0 + jw])
                    P.waittok("gpsimd", tok)
                    if m_cons.get(mi - 4) is not None:
                        P.wait("gpsimd", "cT", m_cons[mi - 4])
                    m_mark[mi] = P.op("gpsimd", lambda e, _m=mi, _r=rslot: e.tensor_copy(
                        mrng[:, _m % 4, :], rs[:, _r, :]), inc="cP")
                    chase["ring_tok"][rslot] = m_mark[mi]

            emit_wcasts(0)
            for j in range(nj):
                j0, jw = c.WJB[j]
                for t in range(c.NT):
                    bank = grp[0] % 4
                    if grp[0] >= 4:
                        P.wait("tensor", "cV", evac_mark[grp[0] - 4])
                    for h in range(c.HC):
                        qidx = qbase[0] + j * c.HC + h
                        if t == 0:
                            P.wait("tensor", "cP", chase["cast_mark"][qidx])
                        m = P.op("tensor", lambda e, _h=h, _t=t, _q=qidx, _b=bank:
                                 e.matmul(ps[:, _b, :], yT[:, _h, 128 * _t:128 * (_t + 1)],
                                          pnl[:, _q % NSLOT, :], start=(_h == 0), stop=False),
                                 inc=("cT" if t == c.NT - 1 else None))
                        if t == c.NT - 1:
                            chase["slot_mark"][qidx] = m
                    P.op("tensor", lambda e, _b=bank, _j0=j0, _jw=jw: e.matmul(
                        ps[:, _b, 0:_jw], ones[:], bbf[:, _j0:_j0 + _jw],
                        start=False, stop=False))
                    gmark = None
                    for ec in range(c.NE):
                        mi = j * c.NE + ec
                        if t == 0:
                            P.wait("tensor", "cP", m_mark[mi])
                        lastc = ec == c.NE - 1
                        inc = None
                        if t == c.NT - 1:
                            inc = "cT"
                        elif lastc:
                            inc = "cM"
                        gmark = P.op("tensor",
                                     lambda e, _e=ec, _t=t, _b=bank, _m=mi, _la=lastc:
                                     e.matmul(ps[:, _b, :], hTb[:, _t, _e, :],
                                              mrng[:, _m % 4, :], start=False, stop=_la),
                                     inc=inc)
                        if t == c.NT - 1:
                            m_cons[mi] = gmark
                    ew = ("cT" if t == c.NT - 1 else "cM", gmark)
                    ob = (j * c.NT + t) % 2
                    if (j * c.NT + t) >= 2:
                        P.waittok("gpsimd", out_tok[j * c.NT + t - 2])
                    tok = P.dma("gpsimd", x1res[:, ob, :],
                                x1D.ap()[128 * t:128 * (t + 1), j0:j0 + jw])
                    P.waittok("vector", tok)
                    P.wait("vector", *ew)
                    evac_mark[grp[0]] = P.op(
                        "vector", lambda e, _b=bank, _ob=ob: e.tensor_add(
                            ostg[:, _ob, :], ps[:, _b, :], x1res[:, _ob, :]), inc="cV")
                    P.wait("gpsimd", "cV", evac_mark[grp[0]])
                    out_tok[j * c.NT + t] = P.dma(
                        "gpsimd", OUT.ap()[128 * t:128 * (t + 1), j0:j0 + jw],
                        ostg[:, ob, :])
                    grp[0] += 1
                if j + 1 < nj:
                    emit_wcasts(j + 1)

        PH = ["ln", "gate", "up", "down", "ln2", "ffn"]
        lvl = PH.index(upto)
        emit_ln(X)
        if dbg:
            P.wait("gpsimd", "cA")
            P.dma("gpsimd", yTdbg.ap(), yT[:].rearrange("p a b -> p (a b)"))
        if lvl >= 1:
            emit_proj(WGT, "gate")
        if lvl >= 2:
            gate_snap = tuple((s, P.cnt[s]) for s in ("sGa", "sGb"))
            build_bisect_all(gate_snap)
            nj = len(c.JB)
            fr = {e: (len(P.defq[e]) + nj - 1) // nj for e in P.defq}
            emit_proj(WUT, "up", pump_fr=fr, fresh=False)
            if dbg and (dbgsel & 1):
                P.wait("gpsimd", "cB")
                P.dma("gpsimd", thrdbg.ap(), thr[:])
        if lvl >= 3:
            emit_down(mtdbg if (dbg and (dbgsel & 2)) else None,
                      x1q0 if (dbg and (dbgsel & 4)) else None)
        if lvl >= 4:
            emit_ln(x1D, after_down=True)
        if lvl >= 5:
            emit_ffn()

        P.qwait("gpsimd", "gpsimd")
        P.qwait("gpsimd", "sync")
        P.wait("gpsimd", "sQt")

        with nc.Block() as block:
            @block.sync
            def _(eng):
                for f in prog.ops["sync"]:
                    f(eng)

            @block.vector
            def _(eng):
                for f in prog.ops["vector"]:
                    f(eng)

            @block.scalar
            def _(eng):
                for f in prog.ops["scalar"]:
                    f(eng)

            @block.tensor
            def _(eng):
                for f in prog.ops["tensor"]:
                    f(eng)

            @block.gpsimd
            def _(eng):
                for f in prog.ops["gpsimd"]:
                    f(eng)

    return nc


def _prep_inputs(L0_attn_x, L0_attn_w_gate, L0_attn_w_up, L0_attn_w_down,
                 L0_ffn_W, L0_ffn_b, L0_ffn_A, L0_ffn_B, L0_ffn_probs, cfg):
    c = cfg
    x = np.ascontiguousarray(np.asarray(L0_attn_x, dtype=np.float32).reshape(-1, c.D))
    shared = {
        "WGT": np.ascontiguousarray(np.asarray(L0_attn_w_gate, np.float32).T),
        "WUT": np.ascontiguousarray(np.asarray(L0_attn_w_up, np.float32).T),
        "WDT": np.ascontiguousarray(np.asarray(L0_attn_w_down, np.float32).T),
        "WT": np.ascontiguousarray(np.asarray(L0_ffn_W, np.float32).T),
        "BIAS": np.asarray(L0_ffn_b, np.float32).reshape(1, -1),
        "ATT": np.ascontiguousarray(
            np.asarray(L0_ffn_A, np.float32).transpose(1, 0, 2).reshape(c.ER, c.D).T),
        "MTT": np.ascontiguousarray(
            np.asarray(L0_ffn_B, np.float32).transpose(2, 0, 1).reshape(c.ER, c.D)),
        "PR": np.asarray(L0_ffn_probs, np.float32),
        "IDT": np.eye(128, dtype=ml_dtypes.bfloat16),
    }
    return x, shared


def kernel(L0_attn_x, L0_attn_w_gate, L0_attn_w_up, L0_attn_w_down,
           L0_ffn_W, L0_ffn_b, L0_ffn_A, L0_ffn_B, L0_ffn_probs):
    cfg = Cfg()
    x, shared = _prep_inputs(L0_attn_x, L0_attn_w_gate, L0_attn_w_up, L0_attn_w_down,
                             L0_ffn_W, L0_ffn_b, L0_ffn_A, L0_ffn_B, L0_ffn_probs, cfg)
    nc = build_core_program(cfg)
    in_maps = []
    for core in range(NCORES):
        m = dict(shared)
        m["X"] = x[core * cfg.ROWS:(core + 1) * cfg.ROWS]
        in_maps.append(m)
    res = run_bass_kernel_spmd(nc, in_maps, core_ids=list(range(NCORES)))
    out = np.concatenate([res.results[cc]["OUT"] for cc in range(NCORES)], axis=0)
    return out.reshape(np.asarray(L0_attn_x).shape).astype(np.float32)
